# version 01 (final)
# baseline (speedup 1.0000x reference)
import contextlib
import numpy as np
import ml_dtypes
import concourse.bass as bass
import concourse.mybir as mybir
from concourse.bass_utils import run_bass_kernel_spmd

F32 = mybir.dt.float32
BF16 = mybir.dt.bfloat16
AF = mybir.ActivationFunctionType
ALU = mybir.AluOpType

P = 128
B = 4
S = 8192
D = 1024
HF = 4096
NC = 8
NSLOT = 8
BF16NP = ml_dtypes.bfloat16

DEBUG = ()
PHASES = 6
P1_PROJ = True
P1_AG = 3
P1_ITERS = 99
P1_KIND9 = False
P1_QSRC_KV = False
P1_NS = True

ENGS = ("sync", "tensor", "vector", "scalar", "gpsimd")


class Plan:
    def __init__(self, cnt):
        self.steps = {e: [] for e in ENGS}
        self.c = cnt

    def emit(self, eng, fn, waits=(), incs=()):
        self.steps[eng].append((tuple(waits), tuple(incs), fn))
        for s, k in incs:
            self.c[s] += k

    def run(self, nc, sems):
        with nc.Block() as blk:
            for eng in ENGS:
                steps = self.steps[eng]
                if not steps:
                    continue

                def body(e, steps=steps):
                    for waits, incs, fn in steps:
                        for s, v in waits:
                            if v > 0:
                                e.wait_ge(sems[s], v)
                        ins = fn(e)
                        for s, k in incs:
                            ins.then_inc(sems[s], k)
                getattr(blk, eng)(body)


def _emit_ln(pl, C, T, slot, src, dst, waits_stats):
    cb = 8 * slot
    csm = T["cols"][:, cb:cb + 1]
    csq = T["cols"][:, cb + 1:cb + 2]
    cmn = T["cols"][:, cb + 2:cb + 3]
    ct = T["cols"][:, cb + 3:cb + 4]
    cw = T["cols"][:, cb + 4:cb + 5]
    cr = T["cols"][:, cb + 5:cb + 6]
    pl.emit("scalar", lambda s, src=src, csm=csm, csq=csq: (
        s.activation(out=T["junk32b"][:], in_=src, func=AF.Copy, accum_out=csm),
        s.activation(out=T["junk32b"][:], in_=src, func=AF.Square, accum_out=csq))[-1],
        waits=waits_stats, incs=[("st", 1)])
    war = T.setdefault("_lnwar", {}).get(slot, 0)
    pl.emit("scalar", lambda s, csm=csm, csq=csq, cmn=cmn, ct=ct, cw=cw: (
        s.activation(out=cmn, in_=csm, func=AF.Identity, scale=T["c_invD"][:]),
        s.activation(out=ct, in_=cmn, func=AF.Square),
        s.activation(out=ct, in_=ct, func=AF.Identity, scale=T["c_n1"][:],
                     bias=T["c_eps"][:]),
        s.activation(out=cw, in_=csq, func=AF.Identity, scale=T["c_invD"][:],
                     bias=ct))[-1],
        waits=[("ln", war)], incs=[("cm", 1)])
    pl.emit("scalar", lambda s, cw=cw, cr=cr: (
        s.activation(out=cr, in_=cw, func=AF.Ln),
        s.activation(out=cr, in_=cr, func=AF.Exp, scale=T["c_nhalf"][:]))[-1],
        incs=[("rs", 1)])
    return cmn, cr


def build_program():
    nc = bass.Bass("TRN2", target_bir_lowering=False, detect_race_conditions=False)

    x_q = nc.dram_tensor("x_q", [B, 1024, D], F32, kind="ExternalInput").ap()
    x_kv = nc.dram_tensor("x_kv", [B, 1024, D], F32, kind="ExternalInput").ap()
    x_mini = nc.dram_tensor("x_mini", [B, P, D], F32, kind="ExternalInput").ap()
    wqT = nc.dram_tensor("wqT", [D, D], F32, kind="ExternalInput").ap()
    wkT = nc.dram_tensor("wkT", [D, D], F32, kind="ExternalInput").ap()
    wvT = nc.dram_tensor("wvT", [D, D], F32, kind="ExternalInput").ap()
    w1T = nc.dram_tensor("w1T", [D, HF], F32, kind="ExternalInput").ap()
    w2T = nc.dram_tensor("w2T", [HF, D], F32, kind="ExternalInput").ap()
    b1r = nc.dram_tensor("b1r", [P, 32], F32, kind="ExternalInput").ap()
    b2r = nc.dram_tensor("b2r", [P, 8], F32, kind="ExternalInput").ap()
    scale1 = nc.dram_tensor("scale1", [P, 1], F32, kind="ExternalInput").ap()
    masks_in = nc.dram_tensor("masks", [NSLOT, 2, P, 512], BF16, kind="ExternalInput").ap()
    mask_mini_in = nc.dram_tensor("mask_mini", [P, 512], BF16, kind="ExternalInput").ap()
    ident_in = nc.dram_tensor("ident", [P, P], BF16, kind="ExternalInput").ap()
    tri16_in = nc.dram_tensor("tri16", [16, 16], BF16, kind="ExternalInput").ap()

    out_sh = nc.dram_tensor("out_sh", [B, 1024, D], F32, kind="ExternalOutput").ap()

    k_sh = nc.dram_tensor("k_sh", [B, 1024, D], BF16, kind="Internal").ap()
    v_sh = nc.dram_tensor("v_sh", [B, 1024, D], BF16, kind="Internal").ap()
    vs_sh = nc.dram_tensor("vs_sh", [B * 2, D], F32, kind="Internal").ap()
    k_full = nc.dram_tensor("k_full", [NC * B, 1024, D], BF16, kind="Internal",
                            addr_space="Shared").ap()
    v_full = nc.dram_tensor("v_full", [NC * B, 1024, D], BF16, kind="Internal",
                            addr_space="Shared").ap()
    vs_full = nc.dram_tensor("vs_full", [NC * B * 2, D], F32, kind="Internal",
                             addr_space="Shared").ap()
    q_sc = nc.dram_tensor("q_sc", [B, 8, P, 1024], BF16, kind="Internal").ap()
    x2_sc = nc.dram_tensor("x2_sc", [B, 1024, D], F32, kind="Internal").ap()
    h_sc = nc.dram_tensor("h_sc", [B, 2, 32, P, 512], BF16, kind="Internal").ap()

    dbg = {}
    for nm, (shape, dt) in dict(
            cols=([8, P, 4], F32),
            n1b=([8, P, D], F32),
            n1f=([17, P, 4096], F32),
            ksh=([B, 1024, D], F32), vsh=([B, 1024, D], F32),
            x2=([B, 1024, D], F32), q=([B, 8, P, 1024], F32),
            kfull=([NC * B, 1024, D], F32), vfull=([NC * B, 1024, D], F32),
            vsfull=([NC * B * 2, D], F32), n3l=([P, B * D], F32),
            h=([B, 2, 32, P, 512], F32)).items():
        if nm in DEBUG:
            dbg[nm] = nc.dram_tensor("d_" + nm, shape, dt, kind="ExternalOutput").ap()

    sem_names = ["w", "x", "st", "cm", "rs", "ln", "tp", "te", "pj", "ev", "wr",
                 "ns", "nse", "vs", "cc", "qq", "kk", "vv", "sc", "ee", "sm",
                 "av", "fl", "sfx", "x2w", "hw", "ow", "ge", "cv", "msk", "ex",
                 "gd", "pk", "x0", "x1", "x2", "x3", "kk0", "kk1", "vv0", "vv1",
                 "wr0", "wr1", "hw0", "hw1", "ow0", "ow1", "xq0", "xq1", "vsl", "vsw", "wg"]
    stack = contextlib.ExitStack()
    sems = {n: stack.enter_context(nc.semaphore("s_" + n)) for n in sem_names}
    C = {n: 0 for n in sem_names}

    T = {}

    def alloc(name, shape, dt):
        T[name] = stack.enter_context(nc.sbuf_tensor(name, shape, dt))
        return T[name]

    alloc("c_eps", [P, 1], F32)
    alloc("c_nhalf", [P, 1], F32)
    alloc("c_invD", [P, 1], F32)
    alloc("c_n1", [P, 1], F32)
    alloc("c_inv32", [P, 1], F32)
    alloc("c_1em9", [P, 1], F32)
    alloc("ones_row", [1, P], BF16)
    alloc("ones_colv", [P, 1], BF16)
    alloc("ident_s", [P, P], BF16)
    alloc("ident_f", [P, P], F32)
    alloc("tri16_s", [16, 16], BF16)
    alloc("scale_c", [P, 1], F32)
    alloc("junk32", [P, D], F32)
    alloc("junk32b", [P, D], F32)
    alloc("junkc", [P, 512], F32)
    alloc("cols", [P, 64], F32)
    alloc("vsum_bf", [16, B * D], BF16)
    alloc("n3l_tm", [P, B * D], BF16)
    alloc("n3l_fm", [P, B * 8 * P], BF16)
    alloc("b1s", [P, 32], F32)
    alloc("b2s", [P, 8], F32)

    with stack:
        pl = Plan(C)
        pl.emit("vector", lambda v: (
            v.memset(T["c_eps"][:], 1e-5), v.memset(T["c_nhalf"][:], -0.5),
            v.memset(T["c_invD"][:], 1.0 / D),
            v.memset(T["c_n1"][:], -1.0), v.memset(T["c_inv32"][:], 1.0 / 32.0),
            v.memset(T["c_1em9"][:], 1e-9), v.memset(T["ones_row"][:], 1.0),
            v.memset(T["ones_colv"][:], 1.0))[-1])
        for dst, src in ((T["ident_s"][:], ident_in), (T["tri16_s"][:], tri16_in),
                         (T["scale_c"][:], scale1), (T["b1s"][:], b1r),
                         (T["b2s"][:], b2r)):
            pl.emit("sync", lambda sy, dst=dst, src=src: sy.dma_start(out=dst, in_=src),
                    incs=[("w", 16)])
        pl.emit("scalar", lambda s: s.copy(T["ident_f"][:], T["ident_s"][:]),
                waits=[("w", C["w"])])
        pl.run(nc, sems)

        with nc.sbuf_tensor("qmini", [P, B * 8 * P], BF16) as qmini:
            if PHASES >= 1:
                _phase1(nc, sems, C, T, dict(
                    d_n1f=dbg.get("n1f"), d_n1b=dbg.get("n1b"), d_cols=dbg.get("cols"),
                x_kv=x_kv, x_q=x_q, x_mini=x_mini, wqT=wqT, wkT=wkT, wvT=wvT,
                k_sh=k_sh, v_sh=v_sh, vs_sh=vs_sh, q_sc=q_sc,
                k_full=k_full, v_full=v_full, vs_full=vs_full, qmini=qmini))
            if PHASES >= 3:
                _phase3(nc, sems, C, T, dict(
                    k_full=k_full, v_full=v_full, vs_full=vs_full, q_sc=q_sc,
                    x_q=x_q, x_mini=x_mini, x2_sc=x2_sc, masks_in=masks_in,
                    mask_mini_in=mask_mini_in, qmini=qmini))

        if PHASES >= 5:
            _phase45(nc, sems, C, T, dict(x2_sc=x2_sc, w1T=w1T, h_sc=h_sc))
        if PHASES >= 6:
            _phase6(nc, sems, C, T, dict(x2_sc=x2_sc, w2T=w2T, h_sc=h_sc, out_sh=out_sh))

        if DEBUG:
            pl = Plan(C)
            srcs = dict(x2=x2_sc, q=q_sc, kfull=k_full, vfull=v_full,
                        vsfull=vs_full, h=h_sc, ksh=k_sh, vsh=v_sh)
            dwaits = [("x2w", C["x2w"]), ("hw0", C["hw0"]), ("hw1", C["hw1"]),
                      ("cc", C["cc"]), ("wr0", C["wr0"]), ("wr1", C["wr1"])]
            for nm in DEBUG:
                if nm in ("n1f", "n1b", "cols"):
                    continue
                src = T["n3l_tm"][:] if nm == "n3l" else srcs[nm]
                pl.emit("gpsimd", lambda g, d=dbg[nm], s=src: g.dma_start(out=d, in_=s),
                        waits=dwaits, incs=[("ow", 16)])
                dwaits = []
            pl.emit("gpsimd", lambda g: g.nop(), waits=[("ow", C["ow"])])
            pl.run(nc, sems)

    return nc



def _phase1(nc, sems, C, T, io):
    _st = contextlib.ExitStack()
    wq_sb = _st.enter_context(nc.sbuf_tensor("wq_sb", [P, 8 * D], BF16))
    wk_sb = _st.enter_context(nc.sbuf_tensor("wk_sb", [P, 8 * D], BF16))
    wv_sb = _st.enter_context(nc.sbuf_tensor("wv_sb", [P, 8 * D], BF16))
    xb = _st.enter_context(nc.sbuf_tensor("xb", [P, 4 * D], F32))
    n1b = _st.enter_context(nc.sbuf_tensor("n1b", [P, 2 * D], BF16))
    n1f = _st.enter_context(nc.sbuf_tensor("n1f", [P, 2 * 4096], BF16))
    nscol = _st.enter_context(nc.sbuf_tensor("nscol", [P, 8], BF16))
    stg = _st.enter_context(nc.sbuf_tensor("stg", [P, 2 * 512], BF16))
    vstage = _st.enter_context(nc.sbuf_tensor("vstage", [1, D], F32))
    ps_tp0 = _st.enter_context(nc.psum_tensor("ps_tp0", [P, 512], BF16))
    ps_tp1 = _st.enter_context(nc.psum_tensor("ps_tp1", [P, 512], BF16))
    ps_tp2 = _st.enter_context(nc.psum_tensor("ps_tp2", [P, 512], BF16))
    ps_tp3 = _st.enter_context(nc.psum_tensor("ps_tp3", [P, 512], BF16))
    ps_pj0 = _st.enter_context(nc.psum_tensor("ps_pj0", [P, 512], F32))
    ps_pj1 = _st.enter_context(nc.psum_tensor("ps_pj1", [P, 512], F32))
    ps_ns = _st.enter_context(nc.psum_tensor("ps_ns", [P, 8], F32))
    ps_vs = _st.enter_context(nc.psum_tensor("ps_vs", [1, 512], F32))
    with _st:
        ps_tp = [ps_tp0, ps_tp1, ps_tp2, ps_tp3]
        ps_pj = [ps_pj0, ps_pj1]
        pl = Plan(C)

        for wsb, wt in ((wq_sb, io["wqT"]), (wk_sb, io["wkT"]), (wv_sb, io["wvT"])):
            pl.emit("gpsimd", lambda g, wsb=wsb, wt=wt: g.dma_start(
                out=wsb[:].rearrange("p (kc d) -> p kc d", kc=8),
                in_=wt.rearrange("(kc p) d -> p kc d", p=P)), incs=[("wg", 16)])
        w_ready = C["wg"]

        its = ([("kv", b, col) for b in range(B) for col in range(2)]
               + [("q", b, col) for b in range(B) for col in range(2)]
               + [("mini", 0, 0)])[:P1_ITERS]
        if P1_KIND9 and len(its) >= 9:
            its[8] = ("kv", 0, 0)

        ln_done, tp_done, te_done, ns_done = {}, {}, {}, {}
        ev_done, wr_done = {}, {}
        vst_wr = [0]
        pjt = 0

        for it, (kind, b, col) in enumerate(its):
            for st in range(4):
                g = 4 * it + st
                if kind == "kv":
                    src = io["x_kv"][b, 512 * col + P * st:512 * col + P * (st + 1), :]
                elif kind == "q":
                    src = io["x_kv" if P1_QSRC_KV else "x_q"][
                        b, 512 * col + P * st:512 * col + P * (st + 1), :]
                else:
                    src = io["x_mini"][st, :, :]
                xs = xb[:, D * st:D * (st + 1)]
                xsem = "x%d" % (g % 4)
                wl = [("ln", ln_done[g - 4])] if g >= 4 else []
                pl.emit("sync", lambda sy, xs=xs, src=src: sy.dma_start(out=xs, in_=src),
                        waits=wl, incs=[(xsem, 16)])
                cmn, cr = _emit_ln(pl, C, T, g % 4, xs, None, [(xsem, C[xsem])])
                n1d = n1b[:, D * (g % 2):D * (g % 2 + 1)]
                wl = [("rs", C["rs"])]
                if g >= 2:
                    wl.append(("tp", tp_done[g - 2]))
                    if ns_done.get(g - 2, 0):
                        wl.append(("ns", ns_done[g - 2]))
                pl.emit("vector", lambda v, n1d=n1d, xs=xs, cmn=cmn, cr=cr: v.tensor_scalar(
                    n1d, xs, cmn, cr, ALU.subtract, ALU.mult), waits=wl, incs=[("ln", 1)])
                ln_done[g] = C["ln"]
                T["_lnwar"][g % 4] = C["ln"]
                if io.get("d_cols") is not None and g < 8:
                    cb_ = 8 * (g % 4)
                    pl.emit("gpsimd", lambda gp, g=g, cb_=cb_: gp.dma_start(
                        out=io["d_cols"][g], in_=T["cols"][:, cb_:cb_ + 4]),
                        waits=[("rs", C["rs"])], incs=[("vsl", 16)])
                if io.get("d_n1b") is not None and g < 8:
                    pl.emit("gpsimd", lambda gp, n1d=n1d, g=g: gp.dma_start(
                        out=io["d_n1b"][g], in_=n1d), waits=[("ln", C["ln"])], incs=[("vsl", 16)])
                wl = [("ln", C["ln"]), ("wg", w_ready)]
                if g >= 2:
                    wl.append(("te", te_done[g - 2]))
                if kind == "kv" and st == 0 and it > 0:
                    wl.append(("nse", C["nse"]))

                def tp_fn(t, n1d=n1d, g=g):
                    ins = None
                    for dc in range(8):
                        bank = ps_tp[(2 * g + dc // 4) % 4]
                        ins = t.transpose(bank[:, P * (dc % 4):P * (dc % 4 + 1)],
                                          n1d[:, P * dc:P * (dc + 1)], T["ident_s"][:])
                    return ins
                pl.emit("tensor", tp_fn, waits=wl, incs=[("tp", 1)])
                tp_done[g] = C["tp"]
                if kind == "kv" and P1_NS:
                    def ns_fn(t, n1d=n1d, st=st):
                        ins = None
                        for dc in range(8):
                            ins = t.matmul(ps_ns[:, dc:dc + 1],
                                           n1d[:, P * dc:P * (dc + 1)], T["ones_colv"][:],
                                           start=(st == 0 and dc == 0),
                                           stop=(st == 3 and dc == 7))
                        return ins
                    pl.emit("tensor", ns_fn, incs=[("ns", 1)])
                    if st == 3:
                        pl.emit("vector", lambda v: v.tensor_copy(nscol[:], ps_ns[:]),
                                waits=[("ns", C["ns"])], incs=[("nse", 1)])
                ns_done[g] = C["ns"]
                nfb = n1f[:, (it % 2) * 4096:(it % 2 + 1) * 4096]

                def ev_fn(v, nfb=nfb, g=g, st=st):
                    ins = None
                    for hb in range(2):
                        bank = ps_tp[(2 * g + 2 * hb) % 4] if False else ps_tp[(2 * g + hb * 1) % 4]
                        bank = ps_tp[(2 * g + hb) % 4]
                        dst = nfb.rearrange("p (dc t) -> p dc t", dc=8)[
                            :, 4 * hb:4 * hb + 4, P * st:P * (st + 1)]
                        ins = v.tensor_copy(dst, bank[:].rearrange("p (dc t) -> p dc t", dc=4))
                    return ins
                pl.emit("vector", ev_fn, waits=[("tp", C["tp"])], incs=[("te", 1)])
                te_done[g] = C["te"]

            if not P1_PROJ:
                continue
            nfb = n1f[:, (it % 2) * 4096:(it % 2 + 1) * 4096]
            te_need = C["te"]
            if io.get("d_n1f") is not None:
                pl.emit("gpsimd", lambda g, nfb=nfb, it=it: g.dma_start(
                    out=io["d_n1f"][it], in_=nfb), waits=[("te", te_need)], incs=[("vsl", 16)])

            def emit_proj(lst, pjt):
                wl = [("te", te_need)]
                if pjt >= 2:
                    wl.append(("ev", ev_done[pjt - 2]))

                def fn(t, lst=tuple(lst)):
                    ins = None
                    for i, (out_ap, lh, rh, st_, sp_) in enumerate(lst):
                        ins = t.matmul(out_ap, lh, rh, start=st_, stop=sp_)
                    return ins
                pl.emit("tensor", fn, waits=wl, incs=[("pj", 1)])

            def emit_evac(pjt, dram, to_sbuf):
                wsem = "wr%d" % (pjt % 2)
                wl = [("pj", C["pj"])]
                if C[wsem]:
                    wl.append((wsem, C[wsem]))
                dst = stg[:, 512 * (pjt % 2):512 * (pjt % 2 + 1)]
                pl.emit("scalar", lambda s, dst=dst, pp=ps_pj[pjt % 2]: s.activation(
                    out=dst, in_=pp[:], func=AF.Copy), waits=wl, incs=[("ev", 1)])
                ev_done[pjt] = C["ev"]
                if dram is not None:
                    pl.emit("sync", lambda sy, dram=dram, dst=dst: sy.dma_start(
                        out=dram, in_=dst), waits=[("ev", C["ev"])], incs=[(wsem, 16)])
                else:
                    pl.emit("scalar", lambda s, dst=dst, to=to_sbuf: s.copy(to, dst),
                            incs=[(wsem, 16)])

            if kind == "kv":
                for wi, wsb in ((0, wk_sb), (1, wv_sb)):
                    for jt in range(4):
                        for eh in range(2):
                            lst = [(ps_pj[pjt % 2][:],
                                    nfb[:, 512 * kc + P * jt:512 * kc + P * (jt + 1)],
                                    wsb[:, D * kc + 512 * eh:D * kc + 512 * (eh + 1)],
                                    kc == 0, kc == 7) for kc in range(8)]
                            emit_proj(lst, pjt)
                            dram = (io["k_sh"] if wi == 0 else io["v_sh"])[
                                b, 512 * col + P * jt:512 * col + P * (jt + 1),
                                512 * eh:512 * (eh + 1)]
                            emit_evac(pjt, dram, None)
                            pjt += 1

                for eh in range(2):
                    if not P1_NS:
                        break
                    def vs_fn(t, eh=eh):
                        ins = None
                        for kc in range(8):
                            ins = t.matmul(ps_vs[:],
                                           nscol[:, kc:kc + 1],
                                           wv_sb[:, D * kc + 512 * eh:D * kc + 512 * (eh + 1)],
                                           start=(kc == 0), stop=(kc == 7))
                        return ins
                    wl = [("nse", C["nse"])]
                    if C["vs"]:
                        wl.append(("vs", C["vs"]))
                    pl.emit("tensor", vs_fn, waits=wl, incs=[("vs", 1)])
                    wl = [("vs", C["vs"])]
                    if C["vsw"]:
                        wl.append(("vsw", C["vsw"]))
                    pl.emit("vector", lambda v, eh=eh: v.tensor_copy(
                        vstage[0:1, 512 * eh:512 * (eh + 1)], ps_vs[:]),
                        waits=wl, incs=[("vs", 1)])
                pl.emit("sync", lambda sy, r=2 * b + col: sy.dma_start(
                    out=io["vs_sh"][r:r + 1, :], in_=vstage[0:1, :]),
                    waits=[("vs", C["vs"])], incs=[("vsw", 16)])
            elif kind == "q":
                for oc in range(8):
                    lst = [(ps_pj[pjt % 2][:],
                            wq_sb[:, D * kc + P * oc:D * kc + P * oc + P],
                            nfb[:, 512 * kc:512 * (kc + 1)],
                            kc == 0, kc == 7) for kc in range(8)]
                    emit_proj(lst, pjt)
                    emit_evac(pjt, io["q_sc"][b, oc, :, 512 * col:512 * (col + 1)], None)
                    pjt += 1
            else:
                for bb in range(4):
                    for ocg in range(2):
                        lst = []
                        for oc4 in range(4):
                            oc = ocg * 4 + oc4
                            for kc in range(8):
                                lst.append((ps_pj[pjt % 2][:, P * oc4:P * (oc4 + 1)],
                                            wq_sb[:, D * kc + P * oc:D * kc + P * oc + P],
                                            nfb[:, 512 * kc + P * bb:512 * kc + P * (bb + 1)],
                                            kc == 0, kc == 7))
                        emit_proj(lst, pjt)
                        emit_evac(pjt, None,
                                  io["qmini"][:, bb * 1024 + 512 * ocg:bb * 1024 + 512 * (ocg + 1)])
                        pjt += 1

            if kind == "kv" and (b, col) == (B - 1, 1) and P1_AG and P1_ITERS >= 8:
                pl.emit("gpsimd", lambda g: g.collective_compute(
                    "AllGather", ALU.bypass, ins=[io["k_sh"]], outs=[io["k_full"]],
                    replica_groups=[list(range(NC))]),
                    waits=[("wr0", C["wr0"]), ("wr1", C["wr1"]), ("vsw", C["vsw"])],
                    incs=[("cc", 1)])
                if P1_AG >= 2:
                    pl.emit("gpsimd", lambda g: g.collective_compute(
                        "AllGather", ALU.bypass, ins=[io["v_sh"]], outs=[io["v_full"]],
                        replica_groups=[list(range(NC))]), incs=[("cc", 1)])
                if P1_AG >= 3:
                    pl.emit("gpsimd", lambda g: g.collective_compute(
                        "AllGather", ALU.bypass, ins=[io["vs_sh"]], outs=[io["vs_full"]],
                        replica_groups=[list(range(NC))]), incs=[("cc", 1)])

        pl.run(nc, sems)



class NS:
    pass


def _p3_chunk(X, n, gslots, ukind, bnds):
    pl, C, T = X.pl, X.C, X.T
    for b in range(B):
        X.kli += 1
        kpar = X.kli % 2
        wl = [("cc", X.cc_all)] if X.kli == 1 else []
        if X.kli > 2:
            wl.append(("sc", X.ku_done[X.kli - 2]))
        gg, cs = n // 2, 512 * (n % 2)

        kv_r = X.k_full.rearrange("rb (i e) d -> rb e i d", e=8)
        ksem = "kk%d" % kpar
        for ic in range(8):
            pl.emit("sync", lambda sy, b=b, gg=gg, cs=cs, kpar=kpar, ic=ic, X=X, kv_r=kv_r:
                    sy.dma_start(
                        out=X.kb[:, kpar * 4096 + 512 * ic:kpar * 4096 + 512 * (ic + 1)],
                        in_=kv_r[ic * B + b, gg, :, cs:cs + 512]),
                    waits=(wl if ic == 0 else []), incs=[(ksem, 16)])
        kk_need = (ksem, C[ksem])

        for s_abs, sl in gslots:
            if ukind == "pair" and n >= bnds[sl]:
                continue
            X.sti += 1
            wl = [kk_need]
            if X.qq_need:
                wl.append(("qq", X.qq_need))
            if X.sti > 2:
                wl.append(("ee", X.ee_done[X.sti - 2]))
            qbase = (b * 1024) if ukind == "mini" else ((sl * B + b) * 1024)
            qsrc = X.qmini if ukind == "mini" else X.qp

            def sc_fn(t, qsrc=qsrc, qbase=qbase, kpar=kpar, sti=X.sti, X=X):
                ins = None
                for dc in range(8):
                    ins = t.matmul(
                        X.ps_s[sti % 2][:],
                        qsrc[:, qbase + P * dc:qbase + P * (dc + 1)],
                        X.kb[:, kpar * 4096 + 512 * dc:kpar * 4096 + 512 * (dc + 1)],
                        start=(dc == 0), stop=(dc == 7))
                return ins
            pl.emit("tensor", sc_fn, waits=wl, incs=[("sc", 1)])
            wl = [("sc", C["sc"])]
            la = X.last_attn.get(sl if ukind == "pair" else "mini", 0)
            if la:
                wl.append(("sm", la))
            pl.emit("scalar", lambda s, sl=sl, b=b, sti=X.sti, X=X: s.activation(
                out=X.eb[:, (sl * B + b) * 512:(sl * B + b + 1) * 512],
                in_=X.ps_s[sti % 2][:], func=AF.Exp, scale=T["c_inv32"][:]),
                waits=wl, incs=[("ee", 1)])
            X.ee_done[X.sti] = C["ee"]
        X.ku_done[X.kli] = C["sc"]

    for s_abs, sl in gslots:
        if ukind == "pair" and n >= bnds[sl]:
            continue
        ee_need = C["ee"]
        stile = X.rb[:, sl * 512:(sl + 1) * 512]
        e_ = tuple(X.eb[:, (sl * B + bb) * 512:(sl * B + bb + 1) * 512] for bb in range(B))
        pl.emit("vector", lambda v, e_=e_, st_=stile: (
            v.tensor_add(st_, e_[0], e_[1]),
            v.tensor_add(st_, st_, e_[2]),
            v.tensor_add(st_, st_, e_[3]))[-1],
            waits=[("ee", ee_need)], incs=[("sm", 1)])
        pl.emit("scalar", lambda s, st_=stile, T=T: (
            s.activation(out=T["junkc"][:], in_=st_, func=AF.Ln),
            s.activation(out=st_, in_=T["junkc"][:], func=AF.Exp,
                         scale=T["c_n1"][:]))[-1],
            waits=[("sm", C["sm"])], incs=[("sm", 1)])
        if ukind == "mini":
            is_diag = True
            mt = X.msk[:, 4 * 512:5 * 512]
            tmt = X.tmsk[:, 4 * 512:5 * 512]
        else:
            is_diag = n >= bnds[sl] - 2
            if is_diag:
                pch = n - (bnds[sl] - 2)
                mt = X.msk[:, (sl * 2 + pch) * 512:(sl * 2 + pch + 1) * 512]
                tmt = X.tmsk[:, (sl * 2 + pch) * 512:(sl * 2 + pch + 1) * 512]
            else:
                mt = tmt = None

        def at_fn(v, sl=sl, is_diag=is_diag, mt=mt, tmt=tmt, st_=stile, e_=e_, X=X):
            ins = None
            for bb in range(B):
                a_ = X.ab[:, (sl * B + bb) * 512:(sl * B + bb + 1) * 512]
                if is_diag:
                    v.tensor_mul(X.vtmp[:], e_[bb], st_)
                    v.tensor_mul(X.vtmp[:], X.vtmp[:], mt)
                    ins = v.tensor_add(a_, X.vtmp[:], tmt)
                else:
                    ins = v.tensor_mul(a_, e_[bb], st_)
            return ins
        wl = [("sm", C["sm"])]
        if X.msk_need:
            wl.append(("msk", X.msk_need))
        prev_tr = X.te3_done.get(("ab", sl, ukind), 0)
        if prev_tr:
            wl.append(("te", prev_tr))
        pl.emit("vector", at_fn, waits=wl, incs=[("sm", 1)])
        X.last_attn[sl if ukind == "pair" else "mini"] = C["sm"]
        sm_need = C["sm"]

        ci = X.chunk_ci
        for b in range(B):
            X.tri += 1
            wl = [("sm", sm_need)]
            if X.tri > 2:
                wl.append(("te", X.te3_done[X.tri - 2]))

            def tp_fn(t, sl=sl, b=b, tri=X.tri, X=X):
                ins = None
                a_ = X.ab[:, (sl * B + b) * 512:(sl * B + b + 1) * 512]
                for jt in range(4):
                    ins = t.transpose(X.ps_t[tri % 2][:, P * jt:P * (jt + 1)],
                                      a_[:, P * jt:P * (jt + 1)], X.T["ident_s"][:])
                return ins
            pl.emit("tensor", tp_fn, waits=wl, incs=[("tp", 1)])
            wl = [("tp", C["tp"])]
            if X.atT_free:
                wl.append(("av", X.atT_free))
            pl.emit("scalar", lambda s, sl=sl, b=b, ci=ci, tri=X.tri, X=X: s.activation(
                out=X.atT[:, ((ci * 2 + sl) * B + b) * 512:((ci * 2 + sl) * B + b + 1) * 512],
                in_=X.ps_t[tri % 2][:], func=AF.Copy),
                waits=wl, incs=[("te", 1)])
            X.te3_done[X.tri] = C["te"]
        X.te3_done[("ab", sl, ukind)] = C["te"]


def _p3_group_av(X, g, chunks, gslots, bnds):
    pl, C, T = X.pl, X.C, X.T
    te_need3 = C["te"]
    for b in range(B):
        for eh in range(2):
            X.vpi += 1
            vpar = X.vpi % 2
            wl = []
            if X.vpi > 2:
                wl.append(("av", X.av_at[X.vpi - 2]))

            for ci2, n2 in enumerate(chunks):
                def vload(sy, b=b, eh=eh, vpar=vpar, ci2=ci2, n2=n2, X=X):
                    r4, js = n2 // 2, 512 * (n2 % 2)
                    vv = X.v_full[r4 * B + b, js:js + 512, 512 * eh:512 * (eh + 1)]
                    return sy.dma_start(
                        out=X.vb[:, vpar * 4096 + ci2 * 2048:
                                 vpar * 4096 + (ci2 + 1) * 2048].rearrange(
                                     "p (jt e) -> p jt e", jt=4),
                        in_=vv.rearrange("(jt p) e -> p jt e", p=P))
                pl.emit("sync", vload, waits=(wl if ci2 == 0 else []),
                        incs=[("vv%d" % vpar, 16)])
            vv_need = ("vv%d" % vpar, C["vv%d" % vpar])

            for s_abs, sl in gslots:
                X.avi += 1
                wl = [("te", te_need3), vv_need]
                if X.avi > 3:
                    wl.append(("fl", X.fl_done[X.avi - 3]))
                has_tail = (g == 0) and (bnds[sl] < 16)
                if has_tail:
                    wl.append(("sfx", X.sfx_need))

                def av_fn(t, sl=sl, b=b, eh=eh, avi=X.avi, vpar=vpar,
                          nch=len(chunks), has_tail=has_tail, X=X):
                    ins = None
                    started = False
                    if has_tail:
                        ins = t.matmul(
                            X.ps_a[avi % 3][:], X.T["ones_row"][:],
                            X.stage8[0:1, (sl * B + b) * 1024 + 512 * eh:
                                     (sl * B + b) * 1024 + 512 * (eh + 1)],
                            start=True, stop=False)
                        started = True
                    nj = nch * 4
                    ji = 0
                    for ci2 in range(nch):
                        for jt in range(4):
                            ji += 1
                            ins = t.matmul(
                                X.ps_a[avi % 3][:],
                                X.atT[:, ((ci2 * 2 + sl) * B + b) * 512 + P * jt:
                                      ((ci2 * 2 + sl) * B + b) * 512 + P * (jt + 1)],
                                X.vb[:, vpar * 4096 + ci2 * 2048 + jt * 512:
                                     vpar * 4096 + ci2 * 2048 + (jt + 1) * 512],
                                start=(not started and ji == 1), stop=(ji == nj))
                    return ins
                pl.emit("tensor", av_fn, waits=wl, incs=[("av", 1)])
                accs = X.acc[:, (sl * B + b) * 1024 + 512 * eh:
                             (sl * B + b) * 1024 + 512 * (eh + 1)]
                if g == 0:
                    pl.emit("vector", lambda v, accs=accs, avi=X.avi, X=X: v.tensor_copy(
                        accs, X.ps_a[avi % 3][:]), waits=[("av", C["av"])], incs=[("fl", 1)])
                else:
                    pl.emit("vector", lambda v, accs=accs, avi=X.avi, X=X: v.tensor_add(
                        accs, accs, X.ps_a[avi % 3][:]), waits=[("av", C["av"])], incs=[("fl", 1)])
                X.fl_done[X.avi] = C["fl"]
            X.av_at[X.vpi] = C["av"]
    X.atT_free = C["av"]


def _p3_epilogue(X, ukind, slots):
    pl, C, T = X.pl, X.C, X.T
    for s_abs, sl in slots:
        for b in range(B):
            accs = X.acc[:, (sl * B + b) * 1024:(sl * B + b + 1) * 1024]
            cmn, cr = _emit_ln(pl, C, T, 4 + (sl * B + b) % 4, accs, None,
                               [("fl", C["fl"])])
            if ukind == "mini":
                xsrc = X.x_mini[b, :, :]
            else:
                xsrc = X.x_q[b, P * s_abs:P * (s_abs + 1), :]
            xslot = X.xqt[:, D * (b % 2):D * (b % 2 + 1)]
            xqsem = "xq%d" % (b % 2)
            wl = [("ln", C["ln"])] if C["ln"] else []
            pl.emit("sync", lambda sy, xslot=xslot, xsrc=xsrc: sy.dma_start(
                out=xslot, in_=xsrc), waits=wl, incs=[(xqsem, 16)])
            wl = [("rs", C["rs"]), (xqsem, C[xqsem])]
            if ukind == "pair" and X.x2w_last:
                wl.append(("x2w", X.x2w_last))
            pl.emit("vector", lambda v, accs=accs, cmn=cmn, cr=cr, xslot=xslot, X=X: (
                v.tensor_scalar(X.x2st[:], accs, cmn, cr, ALU.subtract, ALU.mult),
                v.tensor_add(X.x2st[:], X.x2st[:], xslot))[-1],
                waits=wl, incs=[("ln", 1)])
            T["_lnwar"][4 + (sl * B + b) % 4] = C["ln"]
            if ukind == "pair":
                pl.emit("sync", lambda sy, b=b, s_abs=s_abs, X=X: sy.dma_start(
                    out=X.x2_sc[b, P * s_abs:P * (s_abs + 1), :], in_=X.x2st[:]),
                    waits=[("ln", C["ln"])], incs=[("x2w", 16)])
                X.x2w_last = C["x2w"]
            else:
                c2mn, c2r = _emit_ln(pl, C, T, 4 + (b + 2) % 4, X.x2st[:], None,
                                     [("ln", C["ln"])])
                pl.emit("vector", lambda v, b=b, c2mn=c2mn, c2r=c2r, X=X: v.tensor_scalar(
                    T["n3l_tm"][:, b * D:(b + 1) * D], X.x2st[:], c2mn, c2r,
                    ALU.subtract, ALU.mult), waits=[("rs", C["rs"])], incs=[("ln", 1)])
                T["_lnwar"][4 + (b + 2) % 4] = C["ln"]
                for half in range(2):
                    pl.emit("tensor", lambda t, b=b, half=half, X=X: (
                        [t.transpose(X.ps_t[half][:, P * q4:P * (q4 + 1)],
                                     T["n3l_tm"][:, b * D + P * (4 * half + q4):
                                                 b * D + P * (4 * half + q4 + 1)],
                                     T["ident_s"][:]) for q4 in range(4)][-1]),
                        waits=[("ln", C["ln"]), ("te", C["te"])], incs=[("tp", 1)])
                    pl.emit("scalar", lambda s, b=b, half=half, X=X: s.activation(
                        out=T["n3l_fm"][:].rearrange("p (bb dc t) -> p bb dc t",
                                                     bb=B, dc=8)[:, b, 4 * half:4 * half + 4, :],
                        in_=X.ps_t[half][:].rearrange("p (dc t) -> p dc t", dc=4),
                        func=AF.Copy), waits=[("tp", C["tp"])], incs=[("te", 1)])


def _phase3(nc, sems, C, T, io):
    _st = contextlib.ExitStack()
    qp = _st.enter_context(nc.sbuf_tensor("p3_qp", [P, 2 * B * 8 * P], BF16))
    kb = _st.enter_context(nc.sbuf_tensor("p3_kb", [P, 2 * 8 * 512], BF16))
    vb = _st.enter_context(nc.sbuf_tensor("p3_vb", [P, 2 * 4096], BF16))
    eb = _st.enter_context(nc.sbuf_tensor("p3_eb", [P, 2 * B * 512], F32))
    rb = _st.enter_context(nc.sbuf_tensor("p3_rb", [P, 2 * 512], F32))
    vtmp = _st.enter_context(nc.sbuf_tensor("p3_vtmp", [P, 512], F32))
    ab = _st.enter_context(nc.sbuf_tensor("p3_ab", [P, 2 * B * 512], BF16))
    atT = _st.enter_context(nc.sbuf_tensor("p3_atT", [P, 2 * 2 * B * 4 * P], BF16))
    acc = _st.enter_context(nc.sbuf_tensor("p3_acc", [P, 2 * B * 1024], F32))
    x2st = _st.enter_context(nc.sbuf_tensor("p3_x2st", [P, D], F32))
    xqt = _st.enter_context(nc.sbuf_tensor("p3_xqt", [P, 2 * D], F32))
    msk = _st.enter_context(nc.sbuf_tensor("p3_msk", [P, 5 * 512], BF16))
    tmsk = _st.enter_context(nc.sbuf_tensor("p3_tmsk", [P, 5 * 512], BF16))
    stage8 = _st.enter_context(nc.sbuf_tensor("p3_stage8", [1, 8 * 1024], BF16))
    ps_s0 = _st.enter_context(nc.psum_tensor("p3_ps_s0", [P, 512], F32))
    ps_s1 = _st.enter_context(nc.psum_tensor("p3_ps_s1", [P, 512], F32))
    ps_t0 = _st.enter_context(nc.psum_tensor("p3_ps_t0", [P, 512], BF16))
    ps_t1 = _st.enter_context(nc.psum_tensor("p3_ps_t1", [P, 512], BF16))
    ps_a0 = _st.enter_context(nc.psum_tensor("p3_ps_a0", [P, 512], F32))
    ps_a1 = _st.enter_context(nc.psum_tensor("p3_ps_a1", [P, 512], F32))
    ps_a2 = _st.enter_context(nc.psum_tensor("p3_ps_a2", [P, 512], F32))
    ps_x = _st.enter_context(nc.psum_tensor("p3_ps_x", [1, 512], F32))
    with _st:
        pl = Plan(C)
        X = NS()
        X.pl, X.C, X.T = pl, C, T
        X.qp, X.kb, X.vb, X.eb, X.rb, X.vtmp = qp, kb, vb, eb, rb, vtmp
        X.ab, X.atT, X.acc, X.x2st, X.xqt = ab, atT, acc, x2st, xqt
        X.msk, X.tmsk, X.stage8 = msk, tmsk, stage8
        X.ps_s = [ps_s0, ps_s1]
        X.ps_t = [ps_t0, ps_t1]
        X.ps_a = [ps_a0, ps_a1, ps_a2]
        X.ps_x = ps_x
        X.k_full, X.v_full = io["k_full"], io["v_full"]
        X.x_q, X.x_mini, X.x2_sc = io["x_q"], io["x_mini"], io["x2_sc"]
        X.qmini = io["qmini"]
        X.cc_all = C["cc"]
        X.sti = X.tri = X.kli = X.vpi = X.avi = 0
        X.ee_done, X.ku_done, X.te3_done = {}, {}, {}
        X.fl_done, X.av_at, X.last_attn = {}, {}, {}
        X.atT_free = 0
        X.x2w_last = 0

        first = True
        for b in range(B):
            for ch in range(16):
                r, i = ch // 2, ch % 2
                row = (r * B + b) * 2 + i
                pl.emit("gpsimd", lambda g, ch=ch, row=row, b=b, io=io: g.dma_start(
                    out=T["vsum_bf"][ch:ch + 1, b * D:(b + 1) * D],
                    in_=io["vs_full"][row:row + 1, :]),
                    waits=([("cc", X.cc_all)] if first else []), incs=[("vsl", 16)])
                first = False
        pl.emit("gpsimd", lambda g: g.engine_nop(), waits=[("vsl", C["vsl"])],
                incs=[("sfx", 1)])

        units = [("mini", None)] + [("pair", m) for m in range(4)]
        for ui, (ukind, um) in enumerate(units):
            if ukind == "mini":
                slots = [(None, 0)]
                ngroups = 1
                bnds = {0: 1}
            else:
                m = um
                slots = [(2 * m, 0), (2 * m + 1, 1)]
                ngroups = 2 * m + 2
                bnds = {0: 4 * m + 2, 1: 4 * m + 4}

            if ukind == "mini":
                pl.emit("sync", lambda sy, io=io: sy.dma_start(
                    out=msk[:, 4 * 512:5 * 512], in_=io["mask_mini_in"]), incs=[("msk", 16)])
                pl.emit("vector", lambda v: v.tensor_scalar(
                    tmsk[:, 4 * 512:5 * 512], msk[:, 4 * 512:5 * 512], -1e-9, 1e-9,
                    ALU.mult, ALU.add), waits=[("msk", C["msk"])], incs=[("msk", 1)])
                X.qq_need = 0
            else:
                m = um
                wl = [("ex", C["ex"])] if m > 0 else []

                for sl in range(2):
                    for pch in range(2):
                        pl.emit("sync", lambda sy, m=m, sl=sl, pch=pch, io=io: sy.dma_start(
                            out=msk[:, (sl * 2 + pch) * 512:(sl * 2 + pch + 1) * 512],
                            in_=io["masks_in"][2 * m + sl, pch, :, :]),
                            waits=(wl if (sl, pch) == (0, 0) else []), incs=[("msk", 16)])
                pl.emit("vector", lambda v: v.tensor_scalar(
                    tmsk[:, 0:4 * 512], msk[:, 0:4 * 512], -1e-9, 1e-9, ALU.mult, ALU.add),
                    waits=[("msk", C["msk"])], incs=[("msk", 1)])

                qwl = [("wr0", C["wr0"]), ("wr1", C["wr1"])]
                for sl in range(2):
                    for b in range(B):
                        for dc in range(8):
                            pl.emit("sync", lambda sy, m=m, sl=sl, b=b, dc=dc, io=io:
                                    sy.dma_start(
                                        out=qp[:, (sl * B + b) * 1024 + P * dc:
                                               (sl * B + b) * 1024 + P * (dc + 1)],
                                        in_=io["q_sc"][b, dc, :,
                                                       P * (2 * m + sl):P * (2 * m + sl + 1)]),
                                    waits=qwl, incs=[("qq", 16)])
                            qwl = []
                X.qq_need = C["qq"]
            X.msk_need = C["msk"]

            av_start = C["av"]
            for s_abs, sl in slots:
                for b in range(B):
                    bnd = bnds[sl]
                    if bnd >= 16:
                        continue

                    for eh in range(2):
                        pl.emit("tensor", lambda t, b=b, bnd=bnd, eh=eh, X=X: t.matmul(
                            X.ps_x[:], T["tri16_s"][:, bnd:bnd + 1],
                            T["vsum_bf"][:, b * D + 512 * eh:b * D + 512 * (eh + 1)],
                            start=True, stop=True),
                            waits=[("sfx", C["sfx"]), ("av", av_start)], incs=[("sfx", 1)])
                        pl.emit("scalar", lambda s, sl=sl, b=b, eh=eh, X=X: s.activation(
                            out=stage8[0:1, (sl * B + b) * 1024 + 512 * eh:
                                       (sl * B + b) * 1024 + 512 * (eh + 1)],
                            in_=X.ps_x[:], func=AF.Copy, scale=T["c_1em9"][0:1, :]),
                            waits=[("sfx", C["sfx"])], incs=[("sfx", 1)])
            X.sfx_need = C["sfx"]

            for g in range(ngroups):
                chunks = [0] if ukind == "mini" else [2 * g, 2 * g + 1]
                gslots = [(s_abs, sl) for s_abs, sl in slots
                          if ukind == "mini" or chunks[0] < bnds[sl]]
                for ci, n in enumerate(chunks):
                    X.chunk_ci = ci
                    _p3_chunk(X, n, gslots, ukind, bnds)
                _p3_group_av(X, g, chunks, gslots, bnds)

            _p3_epilogue(X, ukind, slots)
            pl.emit("vector", lambda v: v.engine_nop(), waits=[("av", C["av"])],
                    incs=[("ex", 1)])

        pl.run(nc, sems)



def _phase45(nc, sems, C, T, io):
    x2w_total = C["x2w"]
    _st = contextlib.ExitStack()
    n3f = _st.enter_context(nc.sbuf_tensor("f5_n3f", [P, 8 * 8 * 512], BF16))
    w1_sb = _st.enter_context(nc.sbuf_tensor("f5_w1_sb", [P, 8 * HF], BF16))
    x2b = _st.enter_context(nc.sbuf_tensor("f5_x2b", [P, 4 * D], F32))
    n3t = _st.enter_context(nc.sbuf_tensor("f5_n3t", [P, 2 * D], BF16))
    ev1p = _st.enter_context(nc.sbuf_tensor("f5_ev1p", [1, 4 * 1024], BF16))
    cv1 = _st.enter_context(nc.sbuf_tensor("f5_cv1", [1, 4 * 512], BF16))
    hst = _st.enter_context(nc.sbuf_tensor("f5_hst", [P, 2 * 512], BF16))
    ps_tp0 = _st.enter_context(nc.psum_tensor("f5_ps_tp0", [P, 512], BF16))
    ps_tp1 = _st.enter_context(nc.psum_tensor("f5_ps_tp1", [P, 512], BF16))
    ps_tp2 = _st.enter_context(nc.psum_tensor("f5_ps_tp2", [P, 512], BF16))
    ps_tp3 = _st.enter_context(nc.psum_tensor("f5_ps_tp3", [P, 512], BF16))
    ps_h0 = _st.enter_context(nc.psum_tensor("f5_ps_h0", [P, 512], F32))
    ps_h1 = _st.enter_context(nc.psum_tensor("f5_ps_h1", [P, 512], F32))
    ps_cv = _st.enter_context(nc.psum_tensor("f5_ps_cv", [1, 512], F32))
    with _st:
        ps_tp = [ps_tp0, ps_tp1, ps_tp2, ps_tp3]
        ps_h = [ps_h0, ps_h1]
        pl = Plan(C)
        pl.emit("gpsimd", lambda g: g.dma_start(
            out=w1_sb[:].rearrange("p (kc h) -> p kc h", kc=8),
            in_=io["w1T"].rearrange("(kc p) h -> p kc h", p=P)), incs=[("wg", 16)])
        w1_ready = C["wg"]

        ln_done, tp_done, te_done = {}, {}, {}
        for fc in range(8):
            b, col = fc // 2, fc % 2
            for st in range(4):
                g = 4 * fc + st
                xs = x2b[:, D * st:D * (st + 1)]
                xsem = "x%d" % (g % 4)
                wl = [("x2w", x2w_total)]
                if g >= 4:
                    wl.append(("ln", ln_done[g - 4]))
                pl.emit("sync", lambda sy, xs=xs, b=b, col=col, st=st: sy.dma_start(
                    out=xs, in_=io["x2_sc"][b, 512 * col + P * st:512 * col + P * (st + 1), :]),
                    waits=wl, incs=[(xsem, 16)])
                cmn, cr = _emit_ln(pl, C, T, g % 4, xs, None, [(xsem, C[xsem])])
                n1d = n3t[:, D * (g % 2):D * (g % 2 + 1)]
                wl = [("rs", C["rs"])]
                if g >= 2:
                    wl.append(("tp", tp_done[g - 2]))
                pl.emit("vector", lambda v, n1d=n1d, xs=xs, cmn=cmn, cr=cr: v.tensor_scalar(
                    n1d, xs, cmn, cr, ALU.subtract, ALU.mult), waits=wl, incs=[("ln", 1)])
                ln_done[g] = C["ln"]
                T["_lnwar"][g % 4] = C["ln"]
                if io.get("d_cols") is not None and g < 8:
                    cb_ = 8 * (g % 4)
                    pl.emit("gpsimd", lambda gp, g=g, cb_=cb_: gp.dma_start(
                        out=io["d_cols"][g], in_=T["cols"][:, cb_:cb_ + 4]),
                        waits=[("rs", C["rs"])], incs=[("vsl", 16)])
                if io.get("d_n1b") is not None and g < 8:
                    pl.emit("gpsimd", lambda gp, n1d=n1d, g=g: gp.dma_start(
                        out=io["d_n1b"][g], in_=n1d), waits=[("ln", C["ln"])], incs=[("vsl", 16)])
                wl = [("ln", C["ln"])]
                if g >= 2:
                    wl.append(("te", te_done[g - 2]))

                def tpf(t, n1d=n1d, g=g):
                    ins = None
                    for dc in range(8):
                        bank = ps_tp[(2 * g + dc // 4) % 4]
                        ins = t.transpose(bank[:, P * (dc % 4):P * (dc % 4 + 1)],
                                          n1d[:, P * dc:P * (dc + 1)], T["ident_s"][:])
                    return ins
                pl.emit("tensor", tpf, waits=wl, incs=[("tp", 1)])
                tp_done[g] = C["tp"]

                def evf(v, fc=fc, g=g, st=st):
                    ins = None
                    for hb in range(2):
                        bank = ps_tp[(2 * g + hb) % 4]
                        dst = n3f[:, fc * 4096:(fc + 1) * 4096].rearrange(
                            "p (dc t) -> p dc t", dc=8)[:, 4 * hb:4 * hb + 4, P * st:P * (st + 1)]
                        ins = v.tensor_copy(dst, bank[:].rearrange("p (dc t) -> p dc t", dc=4))
                    return ins
                pl.emit("vector", evf, waits=[("tp", C["tp"])], incs=[("te", 1)])
                te_done[g] = C["te"]

        ge_done, hw_done = {}, {}
        for fc in range(8):
            b, col = fc // 2, fc % 2
            te_need = te_done[4 * fc + 3]
            if col == 0:
                for r in range(4):
                    pl.emit("gpsimd", lambda gp, b=b, r=r: gp.dma_start(
                        out=ev1p[0:1, r * 1024:(r + 1) * 1024],
                        in_=T["n3l_tm"][1 + 2 * r:2 + 2 * r, b * D:(b + 1) * D]),
                        waits=([("ge", C["ge"])] if r == 0 else []), incs=[("pk", 16)])
            pk_need = C["pk"]
            for r in range(4):
                def cvf(t, r=r, fc=fc, b=b):
                    ins = None
                    for kc in range(8):
                        ins = t.matmul(ps_cv[:],
                                       T["n3l_fm"][:, b * 1024 + P * kc + 2 + 2 * r:
                                                   b * 1024 + P * kc + 3 + 2 * r],
                                       n3f[:, fc * 4096 + 512 * kc:fc * 4096 + 512 * (kc + 1)],
                                       start=(kc == 0), stop=(kc == 7))
                    return ins
                wl = [("te", te_need), ("wg", w1_ready)]
                if C["cv"]:
                    wl.append(("cv", C["cv"]))
                pl.emit("tensor", cvf, waits=wl, incs=[("cv", 1)])
                pl.emit("scalar", lambda s, r=r: s.activation(
                    out=cv1[0:1, 512 * r:512 * (r + 1)], in_=ps_cv[:], func=AF.Copy,
                    scale=T["scale_c"][0:1, :]), waits=[("cv", C["cv"])], incs=[("cv", 1)])
            cv_need = C["cv"]
            for hc in range(32):
                r = hc // 8
                hti = fc * 32 + hc

                def hf(t, hc=hc, r=r, fc=fc, hti=hti):
                    for kc in range(8):
                        t.matmul(ps_h[hti % 2][:],
                                 w1_sb[:, HF * kc + P * hc:HF * kc + P * (hc + 1)],
                                 n3f[:, fc * 4096 + 512 * kc:fc * 4096 + 512 * (kc + 1)],
                                 start=(kc == 0), stop=False)
                    return t.matmul(ps_h[hti % 2][:],
                                    ev1p[0:1, r * 1024 + P * (hc % 8):r * 1024 + P * (hc % 8) + P],
                                    cv1[0:1, 512 * r:512 * (r + 1)],
                                    start=False, stop=True)
                wl = [("cv", cv_need), ("pk", pk_need)]
                if hti >= 2:
                    wl.append(("ge", ge_done[hti - 2]))
                pl.emit("tensor", hf, waits=wl, incs=[("pj", 1)])
                hsem = "hw%d" % (hti % 2)
                wl = [("pj", C["pj"])]
                if C[hsem]:
                    wl.append((hsem, C[hsem]))
                pl.emit("scalar", lambda s, hc=hc, hti=hti: s.activation(
                    out=hst[:, 512 * (hti % 2):512 * (hti % 2 + 1)], in_=ps_h[hti % 2][:],
                    func=AF.Gelu, bias=T["b1s"][:, hc:hc + 1]), waits=wl, incs=[("ge", 1)])
                ge_done[hti] = C["ge"]
                pl.emit("sync", lambda sy, b=b, col=col, hc=hc, hti=hti: sy.dma_start(
                    out=io["h_sc"][b, col, hc, :, :],
                    in_=hst[:, 512 * (hti % 2):512 * (hti % 2 + 1)]),
                    waits=[("ge", C["ge"])], incs=[(hsem, 16)])
        pl.run(nc, sems)



def _phase6(nc, sems, C, T, io):
    hw_total_per_fc = 16 * 32
    _st = contextlib.ExitStack()
    w2_sb = _st.enter_context(nc.sbuf_tensor("f6_w2_sb", [P, 32 * D], BF16))
    hbuf = _st.enter_context(nc.sbuf_tensor("f6_hbuf", [P, 32 * 512], BF16))
    x2b = _st.enter_context(nc.sbuf_tensor("f6_x2b", [P, 4 * D], F32))
    ytm = _st.enter_context(nc.sbuf_tensor("f6_ytm", [P, 4 * D], F32))
    ysb = _st.enter_context(nc.sbuf_tensor("f6_ysb", [P, 2 * 512], F32))
    ost = _st.enter_context(nc.sbuf_tensor("f6_ost", [P, 2 * D], F32))
    gsb = _st.enter_context(nc.sbuf_tensor("f6_gsb", [4, 4 * 512], BF16))
    ea0 = _st.enter_context(nc.sbuf_tensor("f6_ea0", [4, D], BF16))
    ea1 = _st.enter_context(nc.sbuf_tensor("f6_ea1", [4, D], BF16))
    ea2 = _st.enter_context(nc.sbuf_tensor("f6_ea2", [4, D], BF16))
    ea3 = _st.enter_context(nc.sbuf_tensor("f6_ea3", [4, D], BF16))
    ps_y0 = _st.enter_context(nc.psum_tensor("f6_ps_y0", [P, 512], F32))
    ps_y1 = _st.enter_context(nc.psum_tensor("f6_ps_y1", [P, 512], F32))
    ps_g = _st.enter_context(nc.psum_tensor("f6_ps_g", [4, 512], F32))
    ps_t0 = _st.enter_context(nc.psum_tensor("f6_ps_t0", [P, 512], F32))
    ps_t1 = _st.enter_context(nc.psum_tensor("f6_ps_t1", [P, 512], F32))
    with _st:
        ps_y = [ps_y0, ps_y1]
        ps_tt = [ps_t0, ps_t1]
        eas = [ea0, ea1, ea2, ea3]
        pl = Plan(C)
        pl.emit("gpsimd", lambda g: g.dma_start(
            out=w2_sb[:].rearrange("p (kc d) -> p kc d", kc=32),
            in_=io["w2T"].rearrange("(kc p) d -> p kc d", p=P)), incs=[("wg", 16)])
        w2_ready = C["wg"]
        pl.emit("vector", lambda v: (v.memset(ea0[:], 0.0), v.memset(ea1[:], 0.0),
                                     v.memset(ea2[:], 0.0), v.memset(ea3[:], 0.0))[-1],
                incs=[("msk", 1)])
        ea_zero = C["msk"]

        def emit_eapk(b, wl):
            for u in range(4):
                for r in range(4):
                    pl.emit("gpsimd", lambda gp, u=u, r=r, b=b: gp.dma_start(
                        out=eas[u][r:r + 1, 256 * r:256 * (r + 1)],
                        in_=T["n3l_tm"][9 + 2 * r:10 + 2 * r, b * D + u:b * D + u + 1021:4],
                        allow_slow_non_contiguous=True),
                        waits=wl, incs=[("pk", 16)])
                    wl = []
            return C["pk"]

        ev_done, te_done, ow_done, gd_hold, pj_after = {}, {}, {}, {}, {}
        yi = 0
        oti = 0
        hw_base = 0
        for fc in range(8):
            b, col = fc // 2, fc % 2
            wl = [("hw0", C["hw0"]), ("hw1", C["hw1"])]
            if fc >= 1:
                wl.append(("pj", pj_after[fc - 1]))
            if fc % 2 == 0:
                ea_ready = emit_eapk(fc // 2, ([("msk", ea_zero)] if fc == 0 else
                                               [("pj", pj_after[fc - 1])]))
            pl.emit("sync", lambda sy, b=b, col=col: sy.dma_start(
                out=hbuf[:].rearrange("p (hc f) -> p hc f", hc=32),
                in_=io["h_sc"][b, col].rearrange("hc p f -> p hc f")),
                waits=wl, incs=[("xq0", 16)])
            h_need = ("xq0", C["xq0"])
            x2_need = []
            for st in range(4):
                wl = [("ln", C["ln"])] if fc >= 1 else []
                pl.emit("sync", lambda sy, b=b, col=col, st=st: sy.dma_start(
                    out=x2b[:, D * st:D * (st + 1)],
                    in_=io["x2_sc"][b, 512 * col + P * st:512 * col + P * (st + 1), :]),
                    waits=wl, incs=[("x%d" % st, 16)])
                x2_need.append(("x%d" % st, C["x%d" % st]))
            for u in range(4):
                def gf(t, u=u, b=b):
                    ins = None
                    for kc in range(8):
                        ins = t.matmul(ps_g[:],
                                       T["n3l_fm"][:, b * 1024 + P * kc + 10:
                                                   b * 1024 + P * kc + 17:2],
                                       hbuf[:, (u * 8 + kc) * 512:(u * 8 + kc + 1) * 512],
                                       start=(kc == 0), stop=(kc == 7))
                    return ins
                wl = [h_need, ("wg", w2_ready)]
                if C["gd"]:
                    wl.append(("gd", C["gd"]))
                pl.emit("tensor", gf, waits=wl, incs=[("gd", 1)])
                pl.emit("scalar", lambda s, u=u: s.activation(
                    out=gsb[:, 512 * u:512 * (u + 1)], in_=ps_g[:], func=AF.Copy,
                    scale=T["scale_c"][0:4, :]), waits=[("gd", C["gd"])], incs=[("gd", 1)])
            gd_need = C["gd"]
            gd_hold[fc] = gd_need
            for oc in range(8):
                yi += 1

                def yf(t, oc=oc, b=b, yi=yi):
                    for kc in range(32):
                        t.matmul(ps_y[yi % 2][:],
                                 w2_sb[:, D * kc + P * oc:D * kc + P * (oc + 1)],
                                 hbuf[:, 512 * kc:512 * (kc + 1)],
                                 start=(kc == 0), stop=False)
                    ins = None
                    for u in range(4):
                        ins = t.matmul(ps_y[yi % 2][:],
                                       eas[u][:, P * oc:P * (oc + 1)],
                                       gsb[:, 512 * u:512 * (u + 1)],
                                       start=False, stop=(u == 3))
                    return ins
                wl = [("gd", gd_need), ("pk", ea_ready)]
                if yi > 2:
                    wl.append(("ev", ev_done[yi - 2]))
                pl.emit("tensor", yf, waits=wl, incs=[("pj", 1)])
                wl = [("pj", C["pj"])]
                if yi > 2:
                    wl.append(("tp", te_done.get(("tp", yi - 2), 0)))
                ypar = yi % 2
                pl.emit("scalar", lambda s, oc=oc, yi=yi, ypar=ypar: s.activation(
                    out=ysb[:, 512 * ypar:512 * (ypar + 1)], in_=ps_y[yi % 2][:],
                    func=AF.Identity, bias=T["b2s"][:, oc:oc + 1]), waits=wl, incs=[("ev", 1)])
                ev_done[yi] = C["ev"]

                def ytf(t, yi=yi, ypar=ypar):
                    ins = None
                    for ts_ in range(4):
                        ins = t.transpose(ps_tt[yi % 2][:, P * ts_:P * (ts_ + 1)],
                                          ysb[:, 512 * ypar + P * ts_:512 * ypar + P * (ts_ + 1)],
                                          T["ident_f"][:])
                    return ins
                wl = [("ev", C["ev"])]
                if yi > 2:
                    wl.append(("te", te_done.get(("te", yi - 2), 0)))
                pl.emit("tensor", ytf, waits=wl, incs=[("tp", 1)])
                te_done[("tp", yi)] = C["tp"]
                pl.emit("vector", lambda v, oc=oc, yi=yi: (
                    [v.tensor_copy(ytm[:, D * ts_ + P * oc:D * ts_ + P * (oc + 1)],
                                   ps_tt[yi % 2][:, P * ts_:P * (ts_ + 1)])
                     for ts_ in range(4)][-1]),
                    waits=[("tp", C["tp"])], incs=[("te", 1)])
                te_done[("te", yi)] = C["te"]
            pj_after[fc] = C["pj"]
            te_need = C["te"]
            for st in range(4):
                oti += 1
                yts = ytm[:, D * st:D * (st + 1)]
                cmn, cr = _emit_ln(pl, C, T, st % 4, yts, None, [("te", te_need)])
                od = ost[:, D * (oti % 2):D * (oti % 2 + 1)]
                osem = "ow%d" % (oti % 2)
                wl = [("rs", C["rs"]), x2_need[st]]
                if C[osem]:
                    wl.append((osem, C[osem]))
                pl.emit("vector", lambda v, od=od, yts=yts, cmn=cmn, cr=cr, st=st: (
                    v.tensor_scalar(od, yts, cmn, cr, ALU.subtract, ALU.mult),
                    v.tensor_add(od, od, x2b[:, D * st:D * (st + 1)]))[-1],
                    waits=wl, incs=[("ln", 1)])
                T["_lnwar"][st % 4] = C["ln"]
                pl.emit("sync", lambda sy, od=od, b=b, col=col, st=st: sy.dma_start(
                    out=io["out_sh"][b, 512 * col + P * st:512 * col + P * (st + 1), :],
                    in_=od), waits=[("ln", C["ln"])], incs=[(osem, 16)])
        pl.emit("sync", lambda sy: sy.nop(),
                waits=[("ow0", C["ow0"]), ("ow1", C["ow1"])])
        pl.run(nc, sems)



def _host_shards(L0_attn_x):
    x = np.ascontiguousarray(np.asarray(L0_attn_x, dtype=np.float32))
    xq_shards, xkv_shards = [], []
    for c in range(NC):
        rows = [x[:, P * (8 * s + c):P * (8 * s + c) + P, :] for s in range(NSLOT)]
        xq_shards.append(np.ascontiguousarray(np.concatenate(rows, axis=1)))
        xkv_shards.append(np.ascontiguousarray(x[:, 1024 * c:1024 * (c + 1), :]))
    x_mini = np.ascontiguousarray(x[:, 0:P, :])
    return xq_shards, xkv_shards, x_mini


def _host_masks():
    jf = np.arange(512)
    tp = np.arange(P)
    out = []
    for c in range(NC):
        m = np.zeros((NSLOT, 2, P, 512), dtype=np.float32)
        for s in range(NSLOT):
            t0 = P * (8 * s + c)
            for pch in range(2):
                j0 = 512 * (2 * s + pch)
                m[s, pch] = ((j0 + jf)[None, :] <= (t0 + tp)[:, None]).astype(np.float32)
        out.append(np.ascontiguousarray(m.astype(BF16NP)))
    mini = np.ascontiguousarray((jf[None, :] <= tp[:, None]).astype(np.float32).astype(BF16NP))
    return out, mini


def kernel(L0_attn_x, L0_attn_Wq, L0_attn_Wk, L0_attn_Wv,
           L0_ffn_W1, L0_ffn_b1, L0_ffn_W2, L0_ffn_b2, L0_ffn_scale):
    xq_shards, xkv_shards, x_mini = _host_shards(L0_attn_x)
    mask_shards, mask_mini = _host_masks()

    wqT = np.ascontiguousarray(np.asarray(L0_attn_Wq, dtype=np.float32).T)
    wkT = np.ascontiguousarray(np.asarray(L0_attn_Wk, dtype=np.float32).T)
    wvT = np.ascontiguousarray(np.asarray(L0_attn_Wv, dtype=np.float32).T)
    w1T = np.ascontiguousarray(np.asarray(L0_ffn_W1, dtype=np.float32).T)
    w2T = np.ascontiguousarray(np.asarray(L0_ffn_W2, dtype=np.float32).T)
    b1r = np.ascontiguousarray(np.asarray(L0_ffn_b1, dtype=np.float32).reshape(32, P).T)
    b2r = np.ascontiguousarray(np.asarray(L0_ffn_b2, dtype=np.float32).reshape(8, P).T)
    scale1 = np.ascontiguousarray(np.broadcast_to(
        np.asarray(L0_ffn_scale, dtype=np.float32).reshape(1, 1), (P, 1)))
    ident = np.ascontiguousarray(np.eye(P, dtype=BF16NP))
    tri16 = np.ascontiguousarray(np.tril(np.ones((16, 16), np.float32)).astype(BF16NP))

    nc = build_program()
    in_maps = []
    for c in range(NC):
        in_maps.append({
            "x_q": xq_shards[c], "x_kv": xkv_shards[c], "x_mini": x_mini,
            "wqT": wqT, "wkT": wkT, "wvT": wvT, "w1T": w1T, "w2T": w2T,
            "b1r": b1r, "b2r": b2r, "scale1": scale1,
            "masks": mask_shards[c], "mask_mini": mask_mini,
            "ident": ident, "tri16": tri16,
        })
    res = run_bass_kernel_spmd(nc, in_maps, core_ids=list(range(NC)))

    out = np.empty((B, S, D), dtype=np.float32)
    for c in range(NC):
        o = res.results[c]["out_sh"]
        for s in range(NSLOT):
            col = 8 * s + c
            out[:, P * col:P * col + P, :] = o[:, P * s:P * s + P, :]
    kernel._last = res
    return out
